# version 19
# baseline (speedup 1.0000x reference)
import math
from contextlib import ExitStack

import numpy as np
import ml_dtypes

import concourse.bass as bass
import concourse.mybir as mybir
from concourse.bass_utils import run_bass_kernel_spmd

F32 = mybir.dt.float32
BF16 = mybir.dt.bfloat16
AF = mybir.ActivationFunctionType
ALU = mybir.AluOpType
AX = mybir.AxisListType

S = 2048
D = 4096
HALF = 1024
NH = 32
NKV = 8
HD = 128
NKC = 32
RSQ = 1.0 / math.sqrt(HD)
EPS = 1e-5
NEG = -1.0e9

SEM_NAMES = (
    "c0", "c0c", "ph",
    "p1x0", "p1x1", "p1b", "p1bf", "p1v", "p1r", "p1bc", "p1ev", "p1n",
    "wld0", "wld1", "mm2", "ev2", "rop", "qstg", "qsp", "sqs", "nmm", "nrm",
    "vtr", "vev", "bnd",
    "q0", "q1", "sc", "msk", "exp", "vdn", "rc", "rb", "rbe", "cev", "csp",
    "crl", "xst0", "xst1", "evd", "ev40", "ev41", "out0", "out1", "dbg",
    "bft0", "bft1", "xaev", "chn",
)

ENGS = ("sync", "gpsimd", "tensor", "vector", "scalar")


class B:
    def __init__(self, upto=6, debug=False):
        self.upto = upto
        self.debug = debug
        nc = bass.Bass("TRN2", target_bir_lowering=False)
        self.nc = nc

        ap = lambda t: t.ap()
        self.x_fm = ap(nc.dram_tensor("x_fm", [D, S], F32, kind="ExternalInput"))
        self.wq = ap(nc.dram_tensor("wq", [D, D], F32, kind="ExternalInput"))
        self.wk = ap(nc.dram_tensor("wk", [D, 1024], F32, kind="ExternalInput"))
        self.wv = ap(nc.dram_tensor("wv", [D, 1024], F32, kind="ExternalInput"))
        self.wo = ap(nc.dram_tensor("wo", [D, D], F32, kind="ExternalInput"))
        self.wf = ap(nc.dram_tensor("wf", [D, D], F32, kind="ExternalInput"))
        self.cosT = ap(nc.dram_tensor("cosT", [HD, S], F32, kind="ExternalInput"))
        self.sinT = ap(nc.dram_tensor("sinT", [HD, S], F32, kind="ExternalInput"))
        self.a_t = ap(nc.dram_tensor("a_t", [D, 256], F32, kind="ExternalInput"))
        self.bf_r = ap(nc.dram_tensor("bf_r", [256, D], F32, kind="ExternalInput"))
        self.bias2d = ap(nc.dram_tensor("bias2d", [128, 32], F32, kind="ExternalInput"))
        self.coef = ap(nc.dram_tensor("coef", [16, 1], F32, kind="ExternalInput"))
        self.scal = ap(nc.dram_tensor("scal", [16, 1], F32, kind="ExternalInput"))
        self.emat = ap(nc.dram_tensor("emat", [16, 256], F32, kind="ExternalInput"))
        self.identc = ap(nc.dram_tensor("identc", [128, 128], F32, kind="ExternalInput"))
        self.maskc = ap(nc.dram_tensor("maskc", [128, 4, 512], F32, kind="ExternalInput"))

        dk = "ExternalOutput" if debug else "Internal"
        self.q_spill = ap(nc.dram_tensor("q_spill", [NH, 128, S], BF16, kind=dk))
        self.ctx0_sp = ap(nc.dram_tensor("ctx0_sp", [NH, 128, HALF], BF16, kind=dk))
        self.xprime = ap(nc.dram_tensor("xprime", [D, S], F32, kind=dk))
        self.out_fm = ap(nc.dram_tensor("out_fm", [D, S], F32, kind="ExternalOutput"))
        if debug:
            self.dbg_ln1 = ap(nc.dram_tensor("dbg_ln1", [D, S], BF16, kind="ExternalOutput"))
            self.dbg_k = ap(nc.dram_tensor("dbg_k", [NKV, 128, S], BF16, kind="ExternalOutput"))
            self.dbg_v = ap(nc.dram_tensor("dbg_v", [NKV, 16, 128, 128], BF16, kind="ExternalOutput"))
            self.dbg_ctx1 = ap(nc.dram_tensor("dbg_ctx1", [NH, 128, HALF], BF16, kind="ExternalOutput"))
            self.dbg_negb = ap(nc.dram_tensor("dbg_negb", [128, 32], F32, kind="ExternalOutput"))
            self.dbg_ln2 = ap(nc.dram_tensor("dbg_ln2", [D, S], BF16, kind="ExternalOutput"))

        st = ExitStack()
        E = st.enter_context
        self.U1 = E(nc.sbuf_tensor("U1", [128, 32768], BF16))
        self.U2 = E(nc.sbuf_tensor("U2", [128, 32768], BF16))
        self.WS = E(nc.sbuf_tensor("WS", [128, 2, NKC, 128], BF16))
        self.U4 = E(nc.sbuf_tensor("U4", [128, 4096], F32))
        self.MR = E(nc.sbuf_tensor("MR", [128, 2048], F32))
        self.XST = E(nc.sbuf_tensor("XST", [128, 2048], F32))
        self.SCB = E(nc.sbuf_tensor("SCB", [128, 2048], BF16))
        self.OSTG = E(nc.sbuf_tensor("OSTG", [128, 1024], BF16))
        self.SQS = E(nc.sbuf_tensor("SQS", [128, 1024], BF16))
        self.MSK = E(nc.sbuf_tensor("MSK", [128, 4, 512], BF16))
        self.Q3 = E(nc.sbuf_tensor("Q3", [128, 4, 512], BF16))
        self.BFT = E(nc.sbuf_tensor("BFT", [128, 2, 2, 128], BF16))
        self.ROWS = E(nc.sbuf_tensor("ROWS", [1, 2048], F32))
        self.BND = E(nc.sbuf_tensor("BND", [1, 256], F32))
        self.NEGB = E(nc.sbuf_tensor("NEGB", [128, 32], F32))
        self.RBS = E(nc.sbuf_tensor("RBS", [128, 2, 512], BF16))
        self.ONESF = E(nc.sbuf_tensor("ONESF", [1, 128], F32))
        self.ONESB = E(nc.sbuf_tensor("ONESB", [128, 1], BF16))
        self.ONES2 = E(nc.sbuf_tensor("ONES2", [128, 128], BF16))
        self.ONESR = E(nc.sbuf_tensor("ONESR", [1, 128], BF16))
        self.IDF = E(nc.sbuf_tensor("IDF", [128, 128], F32))
        self.WEXP = E(nc.sbuf_tensor("WEXP", [128, 2], F32))
        self.BIAS = E(nc.sbuf_tensor("BIAS", [128, 32], F32))
        self.CS16 = E(nc.sbuf_tensor("CS16", [16, 2], F32))
        self.EM = E(nc.sbuf_tensor("EM", [16, 256], F32))
        self.WROW = E(nc.sbuf_tensor("WROW", [16, 1], F32))
        self.PS = E(nc.psum_tensor("PS", [128, 4096], F32))

        self.sems = {n: E(nc.semaphore("s_" + n)) for n in SEM_NAMES}
        self.block = E(nc.Block())
        self.block.sync(lambda e: self.emit("sync", e))
        self.block.gpsimd(lambda e: self.emit("gpsimd", e))
        self.block.tensor(lambda e: self.emit("tensor", e))
        self.block.vector(lambda e: self.emit("vector", e))
        self.block.scalar(lambda e: self.emit("scalar", e))
        st.close()

    def on(self, name):
        return self.me == name

    def INC(self, inst, sem, n=1):
        self.C[sem] += n
        if inst is not None:
            inst.then_inc(self.sems[sem], n)

    def wge(self, sem, val):
        self.eng.wait_ge(self.sems[sem], val)

    def W(self, engine, sem):
        if self.me == engine:
            self.eng.wait_ge(self.sems[sem], self.C[sem])

    def bar(self):
        for name in ENGS:
            self.C["ph"] += 1
            if self.on(name):
                self.eng.drain()
                self.eng.sem_inc(self.sems["ph"], 1)
        self.eng.wait_ge(self.sems["ph"], self.C["ph"])

    def arena(self, U, i):
        return U[:, i * 1024:(i + 1) * 1024]

    def kfm(self, kv):
        return self.U2[:, kv * 2048:(kv + 1) * 2048]

    def vtok(self, kv, tile):
        off = 16384 + (kv * 16 + tile) * 128
        return self.U2[:, off:off + 128]

    def emit(self, me, eng):
        self.me = me
        self.eng = eng
        self.C = {k: 0 for k in SEM_NAMES}

        self.p0()
        self.bar()
        for H in (0, 1):
            self.p_ln(self.x_fm, lambda kc: self.arena(self.U1, kc), H,
                      dbg=self.dbg_ln1 if self.debug else None)
            if self.upto >= 2:
                self.p2(H)
            self.bar()
            if self.upto >= 3:
                self.p3(H)
                if H == 0:
                    self.bar()
        if self.upto >= 2 and self.debug:
            self.p2_dbg()
            self.bar()
        if self.upto >= 4:
            self.p4()
        if self.upto >= 5:
            for H in (0, 1):
                U = self.U1 if H == 0 else self.U2
                self.p_ln(self.xprime, lambda kc, U=U: self.arena(U, kc), H,
                          dbg=self.dbg_ln2 if self.debug else None)
        if self.upto >= 6:
            self.p6()
        self.bar()
        if self.on("sync"):
            self.eng.wait_ge(self.sems["out0"], self.C["out0"])
            self.eng.wait_ge(self.sems["out1"], self.C["out1"])
            self.eng.wait_ge(self.sems["dbg"], self.C["dbg"])

    def p0(self):
        on, INC = self.on, self.INC
        eng = self.eng
        if on("gpsimd"):
            INC(eng.dma_start(self.U4[:, 0:2048], self.cosT[:, :]), "c0", 16)
            INC(eng.dma_start(self.U4[:, 2048:4096], self.sinT[:, :]), "c0", 16)
            INC(eng.dma_start(self.MSK[:, :, :], self.maskc), "c0", 16)
            INC(eng.dma_start(self.IDF[:, :], self.identc), "c0", 16)
            INC(eng.dma_start(self.CS16[:, 0:1], self.coef), "c0", 16)
            INC(eng.dma_start(self.CS16[:, 1:2], self.scal), "c0", 16)
            INC(eng.dma_start(self.EM[:, :], self.emat), "c0", 16)
            INC(eng.dma_start(self.BIAS[:, :], self.bias2d), "c0", 16)
        else:
            for _ in range(8):
                INC(None, "c0", 16)
        if on("vector"):
            INC(eng.memset(self.ONESF[:, :], 1.0), "c0c", 1)
            INC(eng.memset(self.ONESB[:, :], 1.0), "c0c", 1)
            INC(eng.memset(self.ONES2[:, :], 1.0), "c0c", 1)
            INC(eng.memset(self.ONESR[:, :], 1.0), "c0c", 1)
            self.wge("c0", self.C["c0"])
            INC(eng.tensor_mul(self.WROW[:, :], self.CS16[:, 0:1], self.CS16[:, 1:2]), "c0c", 1)
        else:
            for _ in range(5):
                INC(None, "c0c", 1)
        if self.on("tensor"):
            self.wge("c0", self.C["c0"])
            self.wge("c0c", self.C["c0c"])
        for ec in range(2):
            if on("tensor"):
                mm = eng.matmul(self.PS[:, 3072 + ec:3073 + ec],
                                self.EM[:, ec * 128:(ec + 1) * 128],
                                self.WROW[:, :], start=True, stop=True)
                INC(mm, "c0c", 1)
            else:
                INC(None, "c0c", 1)
        self.W("scalar", "c0c")
        if on("scalar"):
            INC(eng.copy(self.WEXP[:, 0:2], self.PS[:, 3072:3074]), "c0c", 1)
        else:
            INC(None, "c0c", 1)

    def p_ln(self, src, dstf, H, dbg=None):
        on, INC = self.on, self.INC
        eng = self.eng
        PS, XST, OSTG, SQS = self.PS, self.XST, self.OSTG, self.SQS
        ROWS, MR, ONESB, ONESF = self.ROWS, self.MR, self.ONESB, self.ONESF
        cols = slice(H * HALF, (H + 1) * HALF)

        xslot = lambda kc: XST[:, (kc % 2) * 1024:(kc % 2 + 1) * 1024]
        sqslot = lambda kc: OSTG[:, :] if kc % 2 == 0 else SQS[:, :]

        base_x = [self.C["p1x0"], self.C["p1x1"]]
        base_b = self.C["p1b"]
        base_bf = self.C["p1bf"]

        for kc in range(NKC):
            sx = "p1x%d" % (kc % 2)
            if on("sync"):
                if kc < 2:
                    self.wge("p1b", base_b)
                    self.wge("p1bc", self.C["p1bc"])
                else:
                    self.wge("p1b", base_b + (kc - 1))
                INC(eng.dma_start(xslot(kc), src[kc * 128:(kc + 1) * 128, cols]), sx, 16)
            else:
                INC(None, sx, 16)
            if on("scalar"):
                self.wge(sx, base_x[kc % 2] + 16 * (kc // 2 + 1))
                if kc == 0:
                    self.wge("p1bf", self.C["p1bf"])
                    self.wge("mm2", self.C["mm2"])
                if kc >= 2:
                    self.wge("p1bf", base_bf + (kc - 1))
                eng.copy(dstf(kc), xslot(kc))
                INC(eng.activation(sqslot(kc), xslot(kc), AF.Square), "p1b", 1)
            else:
                INC(None, "p1b", 1)
            if on("tensor"):
                self.wge("p1b", base_b + (kc + 1))
                if kc == 0:
                    self.wge("p1ev", self.C["p1ev"])
                st, sp = (kc == 0), (kc == NKC - 1)
                xb = dstf(kc)
                eng.matmul(PS[0:1, 0:512], ONESB[:, :], xb[:, 0:512], start=st, stop=sp)
                eng.matmul(PS[0:1, 512:1024], ONESB[:, :], xb[:, 512:1024], start=st, stop=sp)
                eng.matmul(PS[0:1, 1024:1536], ONESB[:, :], sqslot(kc)[:, 0:512], start=st, stop=sp)
                i4 = eng.matmul(PS[0:1, 1536:2048], ONESB[:, :], sqslot(kc)[:, 512:1024],
                                start=st, stop=sp)
                INC(i4, "p1bf", 1)
            else:
                INC(None, "p1bf", 1)

        mrow = ROWS[0:1, 0:1024]
        varr = ROWS[0:1, 1024:2048]
        tmp = XST[0:1, 0:1024]
        rrow = XST[0:1, 1024:2048]
        if on("vector"):
            self.wge("p1bf", base_bf + NKC)
            eng.tensor_scalar_mul(mrow, PS[0:1, 0:1024], 1.0 / D)
            INC(eng.tensor_scalar_mul(varr, PS[0:1, 1024:2048], 1.0 / D), "chn", 1)
            self.wge("chn", self.C["chn"])
            INC(eng.tensor_mul(tmp, mrow, mrow), "chn", 1)
            self.wge("chn", self.C["chn"])
            INC(eng.tensor_sub(varr, varr, tmp), "chn", 1)
            self.wge("chn", self.C["chn"])
            INC(eng.tensor_scalar_add(varr, varr, EPS), "p1v", 1)
        else:
            INC(None, "chn", 1)
            INC(None, "chn", 1)
            INC(None, "chn", 1)
            INC(None, "p1v", 1)
        self.W("scalar", "p1v")
        if on("scalar"):
            INC(eng.activation(varr, varr, AF.Sqrt), "p1r", 1)
        else:
            INC(None, "p1r", 1)
        self.W("vector", "p1r")
        if on("vector"):
            INC(eng.reciprocal(rrow, varr), "p1v", 1)
        else:
            INC(None, "p1v", 1)
        self.W("tensor", "p1v")
        if on("tensor"):
            eng.matmul(PS[:, 2048:2560], ONESF[0:1, :], mrow[:, 0:512], start=True, stop=True)
            eng.matmul(PS[:, 2560:3072], ONESF[0:1, :], mrow[:, 512:1024], start=True, stop=True)
            eng.matmul(PS[:, 0:512], ONESF[0:1, :], rrow[:, 0:512], start=True, stop=True)
            INC(eng.matmul(PS[:, 512:1024], ONESF[0:1, :], rrow[:, 512:1024],
                           start=True, stop=True), "p1bc", 1)
        else:
            INC(None, "p1bc", 1)
        self.W("scalar", "p1bc")
        if on("scalar"):
            eng.copy(MR[:, 0:1024], PS[:, 2048:3072])
            INC(eng.copy(MR[:, 1024:2048], PS[:, 0:1024]), "p1ev", 1)
        else:
            INC(None, "p1ev", 1)

        base_n = self.C["p1n"]
        if on("vector"):
            self.wge("p1ev", self.C["p1ev"])
            i = None
            for kc in range(NKC):
                i = eng.tensor_sub(dstf(kc), dstf(kc), MR[:, 0:1024])
            INC(i, "chn", 1)
            self.wge("chn", self.C["chn"])
            for kc in range(NKC):
                INC(eng.tensor_mul(dstf(kc), dstf(kc), MR[:, 1024:2048]), "p1n", 1)
        else:
            INC(None, "chn", 1)
            for kc in range(NKC):
                INC(None, "p1n", 1)
        if dbg is not None:
            for kc in range(NKC):
                if on("sync"):
                    self.wge("p1n", base_n + (kc + 1))
                    INC(eng.dma_start(dbg[kc * 128:(kc + 1) * 128, cols], dstf(kc)), "dbg", 16)
                else:
                    INC(None, "dbg", 16)
            if on("sync"):
                self.wge("dbg", self.C["dbg"])

    def p2(self, H):
        on, INC = self.on, self.INC
        eng = self.eng
        PS, WS, MR, XST, OSTG, SQS = self.PS, self.WS, self.MR, self.XST, self.OSTG, self.SQS
        cosv = self.U4[:, 0:2048]
        sinv = self.U4[:, 2048:4096]
        cols = slice(H * HALF, (H + 1) * HALF)

        base_wld = [self.C["wld0"], self.C["wld1"]]
        base_mm2 = self.C["mm2"]
        base_ev2 = self.C["ev2"]
        base_vtr = self.C["vtr"]
        base_vev = self.C["vev"]

        def wsrc(d):
            if d < 32:
                return self.wq[:, d * 128:(d + 1) * 128]
            if d < 40:
                return self.wk[:, (d - 32) * 128:(d - 31) * 128]
            return self.wv[:, (d - 40) * 128:(d - 39) * 128]

        if on("gpsimd"):
            for d in range(48):
                if d >= 2:
                    self.wge("mm2", base_mm2 + (d - 1))
                elif d == 0:
                    self.wge("mm2", base_mm2)
                src = wsrc(d).rearrange("(k p) c -> p k c", p=128)
                INC(eng.dma_start(self.WS[:, d % 2, :, :], src), "wld%d" % (d % 2), 16)
        else:
            for d in range(48):
                INC(None, "wld%d" % (d % 2), 16)

        vcount = 0
        qkn = 0
        pending_norm = []

        def sqs_slot(n):
            return self.SCB[:, (n % 2) * 1024:(n % 2 + 1) * 1024]

        def emit_norm(slot, n):
            sq = sqs_slot(n)
            if on("tensor"):
                self.wge("sqs", self.C["sqs"])
                self.wge("nrm", self.C["nrm"])
                eng.matmul(PS[0:1, 2048:2560], self.ONESB[:, :], sq[:, 0:512],
                           start=True, stop=True)
                INC(eng.matmul(PS[0:1, 2560:3072], self.ONESB[:, :], sq[:, 512:1024],
                               start=True, stop=True), "nmm", 1)
            else:
                INC(None, "nmm", 1)
            if on("vector"):
                self.wge("nmm", self.C["nmm"])
                INC(eng.tensor_reduce(self.BND[0:1, slot:slot + 1], PS[0:1, 2048:3072],
                                      AX.X, ALU.max), "nrm", 1)
            else:
                INC(None, "nrm", 1)

        for d in range(48):
            r = d % 2
            region = PS[:, r * 1024:(r + 1) * 1024]
            kind = "q" if d < 32 else ("k" if d < 40 else "v")
            h = d if d < 32 else (d - 32 if d < 40 else d - 40)

            if on("tensor"):
                self.wge("wld%d" % (d % 2), base_wld[d % 2] + 16 * (d // 2 + 1))
                if d == 0:
                    self.wge("p1n", self.C["p1n"])
                    self.wge("p1ev", self.C["p1ev"])
                if d >= 2:
                    self.wge("ev2", base_ev2 + (d - 1))
                for kc in range(NKC):
                    st, sp = (kc == 0), (kc == NKC - 1)
                    lhsT = WS[:, d % 2, kc, :]
                    ln = self.arena(self.U1, kc)
                    eng.matmul(region[:, 0:512], lhsT, ln[:, 0:512], start=st, stop=sp)
                    mmi = eng.matmul(region[:, 512:1024], lhsT, ln[:, 512:1024], start=st, stop=sp)
                INC(mmi, "mm2", 1)
            else:
                INC(None, "mm2", 1)
            if pending_norm:
                emit_norm(*pending_norm.pop(0))

            if kind in ("q", "k"):
                t1 = MR[:, 0:1024]
                t2 = MR[:, 1024:2048]
                u = XST[:, 0:1024]
                if on("vector"):
                    self.wge("mm2", base_mm2 + (d + 1))
                    self.wge("qstg", self.C["qstg"])
                    cv = cosv[:, cols]
                    sv = sinv[:, cols]
                    eng.tensor_mul(t1, region, cv)
                    eng.tensor_mul(t2[0:64, :], region[64:128, :], sv[0:64, :])
                    INC(eng.tensor_mul(t2[64:128, :], region[0:64, :], sv[64:128, :]), "ev2", 1)
                    self.wge("ev2", self.C["ev2"])
                    INC(eng.tensor_sub(u[0:64, :], t1[0:64, :], t2[0:64, :]), "chn", 1)
                    self.wge("chn", self.C["chn"])
                    INC(eng.tensor_add(u[64:128, :], t1[64:128, :], t2[64:128, :]), "rop", 1)
                else:
                    INC(None, "ev2", 1)
                    INC(None, "chn", 1)
                    INC(None, "rop", 1)
                if kind == "q":
                    if on("scalar"):
                        if d == 0:
                            self.wge("p1bf", self.C["p1bf"])
                            self.wge("rb", self.C["rb"])
                        self.wge("rop", self.C["rop"])
                        self.wge("qsp", self.C["qsp"])
                        if qkn >= 2:
                            self.wge("nmm", self.C["nmm"] - 1)
                        INC(eng.activation(OSTG[:, :], u, AF.Copy, scale=RSQ), "qstg", 1)
                        self.wge("qstg", self.C["qstg"])
                        INC(eng.activation(sqs_slot(qkn), OSTG[:, :], AF.Square), "sqs", 1)
                    else:
                        INC(None, "qstg", 1)
                        INC(None, "sqs", 1)
                    if on("sync"):
                        self.wge("qstg", self.C["qstg"])
                        INC(eng.dma_start(self.q_spill[h, :, cols], OSTG[:, :]), "qsp", 16)
                    else:
                        INC(None, "qsp", 16)
                else:
                    if on("scalar"):
                        self.wge("rop", self.C["rop"])
                        if qkn >= 2:
                            self.wge("nmm", self.C["nmm"] - 1)
                        INC(eng.activation(self.kfm(h)[:, cols], u, AF.Copy), "qstg", 1)
                        self.wge("qstg", self.C["qstg"])
                        INC(eng.activation(sqs_slot(qkn), self.kfm(h)[:, cols], AF.Square),
                            "sqs", 1)
                    else:
                        INC(None, "qstg", 1)
                        INC(None, "sqs", 1)
                slot = (h * 2 + H) if kind == "q" else (64 + h * 2 + H)
                pending_norm.append((slot, qkn))
                qkn += 1
            else:
                if on("scalar"):
                    self.wge("mm2", base_mm2 + (d + 1))
                    self.wge("qstg", self.C["qstg"])
                    INC(eng.copy(XST[:, 0:1024], region), "ev2", 1)
                else:
                    INC(None, "ev2", 1)
                for tt in range(8):
                    trb = self.PS[:, 2048 + (tt % 2) * 512:2048 + (tt % 2) * 512 + 128]
                    if on("tensor"):
                        self.wge("ev2", self.C["ev2"])
                        if vcount == 0 and tt == 0:
                            self.wge("nrm", self.C["nrm"])
                        ti = vcount * 8 + tt
                        if ti >= 2:
                            self.wge("vev", base_vev + (ti - 1))
                        INC(eng.transpose(trb, XST[:, tt * 128:(tt + 1) * 128], self.IDF[:, :]),
                            "vtr", 1)
                    else:
                        INC(None, "vtr", 1)
                    if on("scalar"):
                        self.wge("vtr", self.C["vtr"])
                        INC(eng.copy(self.vtok(h, H * 8 + tt), trb), "vev", 1)
                    else:
                        INC(None, "vev", 1)
                vcount += 1
        while pending_norm:
            emit_norm(*pending_norm.pop(0))
        if on("sync"):
            self.wge("qsp", self.C["qsp"])

    def p2_dbg(self):
        on, INC = self.on, self.INC
        eng = self.eng
        if on("sync"):
            for kv in range(NKV):
                INC(eng.dma_start(self.dbg_k[kv], self.kfm(kv)), "dbg", 16)
                for t in range(16):
                    INC(eng.dma_start(self.dbg_v[kv, t], self.vtok(kv, t)), "dbg", 16)
            INC(eng.dma_start(self.dbg_negb, self.NEGB[:, :]), "dbg", 16)
            self.wge("dbg", self.C["dbg"])
        else:
            for _ in range(NKV * 17 + 1):
                INC(None, "dbg", 16)

    def p3(self, H):
        on, INC = self.on, self.INC
        eng = self.eng
        PS, SCB, Q3, MSK, RBS = self.PS, self.SCB, self.Q3, self.MSK, self.RBS
        BND, NEGB, ROWS = self.BND, self.NEGB, self.ROWS
        RCB = self.SQS

        self.W("vector", "nrm")
        if on("vector"):
            i = None
            for kv in range(NKV):
                kslot = 64 + kv * 2 + H
                if H == 0:
                    i = eng.tensor_copy(BND[0:1, 80 + kv:81 + kv], BND[0:1, kslot:kslot + 1])
                else:
                    i = eng.tensor_max(BND[0:1, 80 + kv:81 + kv], BND[0:1, 80 + kv:81 + kv],
                                       BND[0:1, kslot:kslot + 1])
            INC(i, "chn", 1)
            self.wge("chn", self.C["chn"])
            for h in range(NH):
                qslot = h * 2 + H
                i = eng.tensor_mul(BND[0:1, 96 + h:97 + h], BND[0:1, qslot:qslot + 1],
                                   BND[0:1, 80 + h // 4:81 + h // 4])
            INC(i, "chn", 1)
            self.wge("chn", self.C["chn"])
            INC(eng.tensor_reduce(BND[0:1, 192:193], BND[0:1, 96:128], AX.X, ALU.max),
                "bnd", 1)
        else:
            INC(None, "chn", 1)
            INC(None, "chn", 1)
            INC(None, "bnd", 1)
        self.W("scalar", "bnd")
        if on("scalar"):
            INC(eng.activation(BND[0:1, 193:194], BND[0:1, 192:193], AF.Sqrt), "chn", 1)
            self.wge("chn", self.C["chn"])
            INC(eng.mul(BND[0:1, 194:195], BND[0:1, 193:194], -1.0), "bnd", 1)
        else:
            INC(None, "chn", 1)
            INC(None, "bnd", 1)
        self.W("tensor", "bnd")
        if on("tensor"):
            INC(eng.matmul(PS[:, 3072:3073], self.ONESF[0:1, :], BND[0:1, 194:195],
                           start=True, stop=True), "bnd", 1)
        else:
            INC(None, "bnd", 1)
        self.W("scalar", "bnd")
        if on("scalar"):
            INC(eng.copy(NEGB[:, 0:1], PS[:, 3072:3073]), "bnd", 1)
        else:
            INC(None, "bnd", 1)
        self.W("tensor", "bnd")
        self.W("scalar", "bnd")

        base_q = [self.C["q0"], self.C["q1"]]
        base_cev = self.C["cev"]
        base_rc = self.C["rc"]
        base_rb = self.C["rb"]
        base_rbe = self.C["rbe"]
        base_csp = self.C["csp"]
        sc_tot = []
        pends = []

        def rcb_row(g, head):
            t = self.SQS if g % 2 == 0 else self.OSTG
            return t[0:1, head * 512:(head + 1) * 512]

        def emit_tail_pe(g):
            gi, h0, h1, jj = g
            if on("tensor"):
                self.wge("rc", base_rc + 2 * (gi + 1))
                self.wge("cev", base_cev + min(2 * (gi + 2), 2 * NH))
                if gi >= 1:
                    self.wge("rbe", base_rbe + 2 * gi)
                INC(eng.matmul(PS[:, 2048:2560], self.ONESR[0:1, :], rcb_row(gi, 0),
                               start=True, stop=True), "rb", 1)
                INC(eng.matmul(PS[:, 2560:3072], self.ONESR[0:1, :], rcb_row(gi, 1),
                               start=True, stop=True), "rb", 1)
            else:
                INC(None, "rb", 1)
                INC(None, "rb", 1)

        def emit_tail_dve(g):
            gi, h0, h1, jj = g
            d0 = self.arena(self.U1, h0)[:, jj * 512:(jj + 1) * 512]
            d1 = self.arena(self.U1, h1)[:, jj * 512:(jj + 1) * 512]
            if on("vector"):
                self.wge("rb", base_rb + 2 * (gi + 1))
                self.wge("nmm", self.C["nmm"])
                INC(eng.tensor_copy(RBS[:, 0, :], PS[:, 2048:2560]), "rbe", 1)
                INC(eng.tensor_copy(RBS[:, 1, :], PS[:, 2560:3072]), "rbe", 1)
                self.wge("rbe", self.C["rbe"])
                self.wge("cev", base_cev + 2 * (gi + 1))
                eng.tensor_mul(d0, d0, RBS[:, 0, :])
                INC(eng.tensor_mul(d1, d1, RBS[:, 1, :]), "nmm", 1)
            else:
                INC(None, "rbe", 1)
                INC(None, "rbe", 1)
                INC(None, "nmm", 1)
            if H == 0:
                if on("sync"):
                    self.wge("nmm", self.C["nmm"])
                    INC(eng.dma_start(self.ctx0_sp[h0, :, jj * 512:(jj + 1) * 512], d0),
                        "csp", 16)
                    INC(eng.dma_start(self.ctx0_sp[h1, :, jj * 512:(jj + 1) * 512], d1),
                        "csp", 16)
                else:
                    INC(None, "csp", 16)
                    INC(None, "csp", 16)

        idx = 0
        for hp in range(NH // 2):
            kv = hp // 2
            h0, h1 = 2 * hp, 2 * hp + 1
            for jj in range(2):
                q0 = H * 1024 + jj * 512
                kcn = q0 // 128 + 4
                p = idx % 2
                qs0 = Q3[:, p * 2, :]
                qs1 = Q3[:, p * 2 + 1, :]
                sq = "q%d" % p

                if on("gpsimd"):
                    if idx >= 2:
                        self.wge("sc", sc_tot[idx - 2])
                    INC(eng.dma_start(qs0, self.q_spill[h0, :, q0:q0 + 512]), sq, 16)
                    INC(eng.dma_start(qs1, self.q_spill[h1, :, q0:q0 + 512]), sq, 16)
                else:
                    INC(None, sq, 16)
                    INC(None, sq, 16)

                base_sc = self.C["sc"]
                base_exp = self.C["exp"]
                base_vdn = self.C["vdn"]
                diag0 = q0 // 128

                def region(kc):
                    return PS[:, (kc % 2) * 1024:(kc % 2 + 1) * 1024]

                def emit_scores(kc):
                    if on("tensor"):
                        if kc == 0:
                            self.wge(sq, base_q[p] + 32 * (idx // 2 + 1))
                        if kc >= 2:
                            self.wge("exp", base_exp + (kc - 1))
                        lhsT = self.kfm(kv)[:, kc * 128:(kc + 1) * 128]
                        INC(eng.matmul(region(kc)[:, 0:512], lhsT, qs0,
                                       start=True, stop=True), "sc", 1)
                        INC(eng.matmul(region(kc)[:, 512:1024], lhsT, qs1,
                                       start=True, stop=True), "sc", 1)
                    else:
                        INC(None, "sc", 1)
                        INC(None, "sc", 1)

                emit_scores(0)
                if kcn > 1:
                    emit_scores(1)
                if len(pends) == 2:
                    g = pends.pop(0)
                    emit_tail_pe(g)
                    emit_tail_dve(g)
                for kc in range(kcn):
                    isdiag = kc >= diag0
                    if isdiag:
                        if on("vector"):
                            self.wge("sc", base_sc + 2 * (kc + 1))
                            mi = kc - diag0
                            eng.tensor_add(region(kc)[:, 0:512], region(kc)[:, 0:512],
                                           MSK[:, mi, :])
                            INC(eng.tensor_add(region(kc)[:, 512:1024],
                                               region(kc)[:, 512:1024], MSK[:, mi, :]),
                                "msk", 1)
                        else:
                            INC(None, "msk", 1)
                    if on("scalar"):
                        if isdiag:
                            self.wge("msk", self.C["msk"])
                        else:
                            self.wge("sc", base_sc + 2 * (kc + 1))
                        if kc >= 2:
                            self.wge("vdn", base_vdn + 2 * (kc - 1))
                        INC(eng.activation(SCB[:, (kc % 2) * 1024:(kc % 2 + 1) * 1024],
                                           region(kc), AF.Exp,
                                           bias=NEGB[:, 0:1], scale=1.0), "exp", 1)
                    else:
                        INC(None, "exp", 1)
                    if on("tensor"):
                        self.wge("exp", base_exp + (kc + 1))
                        if kc == 0:
                            self.wge("rc", base_rc + 2 * idx)
                            self.wge("cev", base_cev + 2 * idx)
                            if idx >= 1:
                                self.wge("rbe", base_rbe + 2 * (idx - 1))
                        st, sp = (kc == 0), (kc == kcn - 1)
                        e0 = SCB[:, (kc % 2) * 1024:(kc % 2) * 1024 + 512]
                        e1 = SCB[:, (kc % 2) * 1024 + 512:(kc % 2 + 1) * 1024]
                        eng.matmul(PS[:, 3072:3584], self.ONES2[:, :], e0, start=st, stop=sp)
                        eng.matmul(PS[:, 3584:4096], self.ONES2[:, :], e1, start=st, stop=sp)
                        vt = self.vtok(kv, kc)
                        INC(eng.matmul(PS[:, 2048:2560], vt, e0, start=st, stop=sp), "vdn", 1)
                        INC(eng.matmul(PS[:, 2560:3072], vt, e1, start=st, stop=sp), "vdn", 1)
                    else:
                        INC(None, "vdn", 1)
                        INC(None, "vdn", 1)
                    if kc + 2 < kcn:
                        emit_scores(kc + 2)
                sc_tot.append(self.C["sc"])

                d0 = self.arena(self.U1, h0)[:, jj * 512:(jj + 1) * 512]
                d1 = self.arena(self.U1, h1)[:, jj * 512:(jj + 1) * 512]
                if on("scalar"):
                    self.wge("vdn", base_vdn + 2 * kcn)
                    INC(eng.copy(d0, PS[:, 2048:2560]), "cev", 1)
                    INC(eng.copy(d1, PS[:, 2560:3072]), "cev", 1)
                else:
                    INC(None, "cev", 1)
                    INC(None, "cev", 1)
                rc0 = ROWS[0:1, 0:512]
                rc1 = ROWS[0:1, 512:1024]
                if on("vector"):
                    self.wge("vdn", base_vdn + 2 * kcn)
                    INC(eng.reciprocal(rc0, PS[0:1, 3072:3584]), "chn", 1)
                    INC(eng.reciprocal(rc1, PS[0:1, 3584:4096]), "chn", 1)
                    self.wge("chn", self.C["chn"])
                    INC(eng.tensor_copy(rcb_row(idx, 0), rc0), "rc", 1)
                    INC(eng.tensor_copy(rcb_row(idx, 1), rc1), "rc", 1)
                else:
                    INC(None, "chn", 1)
                    INC(None, "chn", 1)
                    INC(None, "rc", 1)
                    INC(None, "rc", 1)
                pends.append((idx, h0, h1, jj))
                idx += 1
        for g in pends:
            emit_tail_pe(g)
            emit_tail_dve(g)
        pends = []

        if H == 1 and self.debug:
            if on("sync"):
                for h in range(NH):
                    self.wge("cev", self.C["cev"])
                    INC(eng.dma_start(self.dbg_ctx1[h], self.arena(self.U1, h)), "dbg", 16)
            else:
                for h in range(NH):
                    INC(None, "dbg", 16)
        if on("sync"):
            self.wge("csp", self.C["csp"])
            self.wge("dbg", self.C["dbg"])

    def p4(self):
        on, INC = self.on, self.INC
        eng = self.eng
        PS, WS, XST, U4 = self.PS, self.WS, self.XST, self.U4

        base_wld = [self.C["wld0"], self.C["wld1"]]
        base_mm2 = self.C["mm2"]
        base_evd = self.C["evd"]
        base_xst = [self.C["xst0"], self.C["xst1"]]
        base_ev4 = [self.C["ev40"], self.C["ev41"]]

        if on("gpsimd"):
            self.wge("vdn", self.C["vdn"])
            self.wge("csp", self.C["csp"])
            for h in range(NH):
                INC(eng.dma_start(self.arena(self.U2, h), self.ctx0_sp[h]), "crl", 16)
            for d in range(32):
                if d >= 2:
                    self.wge("mm2", base_mm2 + 2 * (d - 1))
                elif d == 0:
                    self.wge("mm2", base_mm2)
                src = self.wo[:, d * 128:(d + 1) * 128].rearrange("(k p) c -> p k c", p=128)
                INC(eng.dma_start(WS[:, d % 2, :, :], src), "wld%d" % (d % 2), 16)
        else:
            for h in range(NH):
                INC(None, "crl", 16)
            for d in range(32):
                INC(None, "wld%d" % (d % 2), 16)

        idx = 0
        for d in range(32):
            for th in range(2):
                r = idx % 2
                region = PS[:, r * 1024:(r + 1) * 1024]
                cols = slice(th * HALF, (th + 1) * HALF)
                if on("tensor"):
                    self.wge("wld%d" % (d % 2), base_wld[d % 2] + 16 * (d // 2 + 1))
                    if idx == 0:
                        self.wge("crl", self.C["crl"])
                        self.wge("nmm", self.C["nmm"])
                    if idx >= 2:
                        self.wge("evd", base_evd + (idx - 1))
                    for kc in range(NKC):
                        st, sp = (kc == 0), (kc == NKC - 1)
                        lhsT = WS[:, d % 2, kc, :]
                        rhs = self.arena(self.U2 if th == 0 else self.U1, kc)
                        eng.matmul(region[:, 0:512], lhsT, rhs[:, 0:512], start=st, stop=sp)
                        mmi = eng.matmul(region[:, 512:1024], lhsT, rhs[:, 512:1024],
                                         start=st, stop=sp)
                    INC(mmi, "mm2", 1)
                else:
                    INC(None, "mm2", 1)
                xs = XST[:, r * 1024:(r + 1) * 1024]
                if on("sync"):
                    if idx >= 2:
                        self.wge("evd", base_evd + (idx - 1))
                    INC(eng.dma_start(xs, self.x_fm[d * 128:(d + 1) * 128, cols]), "xst%d" % r, 16)
                else:
                    INC(None, "xst%d" % r, 16)
                ev = U4[:, r * 1024:(r + 1) * 1024]
                if on("vector"):
                    self.wge("mm2", base_mm2 + (idx + 1))
                    self.wge("xst%d" % r, base_xst[r] + 16 * (idx // 2 + 1))
                    if idx >= 2:
                        self.wge("ev4%d" % r, base_ev4[r] + 16 * (idx // 2))
                    INC(eng.tensor_add(ev, region, xs), "evd", 1)
                else:
                    INC(None, "evd", 1)
                if on("sync"):
                    self.wge("evd", base_evd + (idx + 1))
                    INC(eng.dma_start(self.xprime[d * 128:(d + 1) * 128, cols], ev), "ev4%d" % r, 16)
                else:
                    INC(None, "ev4%d" % r, 16)
                idx += 1
        if on("sync"):
            self.wge("ev40", self.C["ev40"])
            self.wge("ev41", self.C["ev41"])

    def p6(self):
        on, INC = self.on, self.INC
        eng = self.eng
        PS, WS, XST, U4, BFT = self.PS, self.WS, self.XST, self.U4, self.BFT

        if on("gpsimd"):
            self.wge("mm2", self.C["mm2"])
            for ec in range(2):
                src = self.a_t[:, ec * 128:(ec + 1) * 128].rearrange("(k p) e -> p k e", p=128)
                INC(eng.dma_start(WS[:, ec, :, :], src), "wld%d" % ec, 16)
        else:
            for ec in range(2):
                INC(None, "wld%d" % ec, 16)
        for ec in range(2):
            if on("tensor"):
                self.wge("wld0", self.C["wld0"])
                self.wge("wld1", self.C["wld1"])
                if ec == 0:
                    self.wge("p1n", self.C["p1n"])
                    self.wge("p1ev", self.C["p1ev"])
                if ec == 1:
                    self.wge("xaev", self.C["xaev"])
                for kc in range(NKC):
                    st, sp = (kc == 0), (kc == NKC - 1)
                    lhsT = WS[:, ec, kc, :]
                    for th in range(2):
                        rhs = self.arena(self.U1 if th == 0 else self.U2, kc)
                        eng.matmul(PS[:, th * 1024:th * 1024 + 512], lhsT, rhs[:, 0:512],
                                   start=st, stop=sp)
                        mmi = eng.matmul(PS[:, th * 1024 + 512:(th + 1) * 1024], lhsT,
                                         rhs[:, 512:1024], start=st, stop=sp)
                INC(mmi, "mm2", 1)
            else:
                INC(None, "mm2", 1)
            if on("vector"):
                self.wge("mm2", self.C["mm2"])
                xa_out = self.SCB[:, 0:2048] if ec == 0 else self.MSK[:, :, :]
                INC(eng.tensor_scalar(xa_out, PS[:, 0:2048], self.WEXP[:, ec:ec + 1],
                                      None, ALU.mult), "xaev", 1)
            else:
                INC(None, "xaev", 1)

        base_wld = [self.C["wld0"], self.C["wld1"]]
        base_mm2 = self.C["mm2"]
        base_bft = [self.C["bft0"], self.C["bft1"]]
        base_evd = self.C["evd"]
        base_xst = [self.C["xst0"], self.C["xst1"]]
        base_out = [self.C["out0"], self.C["out1"]]

        if on("gpsimd"):
            for d in range(32):
                if d < 2:
                    self.wge("xaev", self.C["xaev"])
                else:
                    self.wge("mm2", base_mm2 + 2 * (d - 1))
                src = self.wf[:, d * 128:(d + 1) * 128].rearrange("(k p) c -> p k c", p=128)
                INC(eng.dma_start(WS[:, d % 2, :, :], src), "wld%d" % (d % 2), 16)
                bsrc = self.bf_r[:, d * 128:(d + 1) * 128].rearrange("(e p) c -> p e c", p=128)
                INC(eng.dma_start(BFT[:, d % 2, :, :], bsrc), "bft%d" % (d % 2), 16)
        else:
            for d in range(32):
                INC(None, "wld%d" % (d % 2), 16)
                INC(None, "bft%d" % (d % 2), 16)

        idx = 0
        for d in range(32):
            for th in range(2):
                r = idx % 2
                region = PS[:, r * 1024:(r + 1) * 1024]
                cols = slice(th * HALF, (th + 1) * HALF)
                if on("tensor"):
                    self.wge("wld%d" % (d % 2), base_wld[d % 2] + 16 * (d // 2 + 1))
                    self.wge("bft%d" % (d % 2), base_bft[d % 2] + 16 * (d // 2 + 1))
                    if idx == 0:
                        self.wge("xaev", self.C["xaev"])
                    if idx >= 2:
                        self.wge("evd", base_evd + (idx - 1))
                    for kc in range(NKC):
                        st = (kc == 0)
                        lhsT = WS[:, d % 2, kc, :]
                        rhs = self.arena(self.U1 if th == 0 else self.U2, kc)
                        eng.matmul(region[:, 0:512], lhsT, rhs[:, 0:512], start=st, stop=False)
                        eng.matmul(region[:, 512:1024], lhsT, rhs[:, 512:1024],
                                   start=st, stop=False)
                    for ec in range(2):
                        sp = (ec == 1)
                        lhsT = BFT[:, d % 2, ec, :]
                        xa0 = (self.SCB[:, th * 1024:th * 1024 + 512] if ec == 0
                               else self.MSK[:, th * 2, :])
                        xa1 = (self.SCB[:, th * 1024 + 512:(th + 1) * 1024] if ec == 0
                               else self.MSK[:, th * 2 + 1, :])
                        eng.matmul(region[:, 0:512], lhsT, xa0, start=False, stop=sp)
                        mmi = eng.matmul(region[:, 512:1024], lhsT, xa1, start=False, stop=sp)
                    INC(mmi, "mm2", 1)
                else:
                    INC(None, "mm2", 1)
                xs = XST[:, r * 1024:(r + 1) * 1024]
                if on("sync"):
                    if idx == 0:
                        self.wge("p1bc", self.C["p1bc"])
                        self.wge("p1b", self.C["p1b"])
                    if idx >= 2:
                        self.wge("evd", base_evd + (idx - 1))
                    INC(eng.dma_start(xs, self.xprime[d * 128:(d + 1) * 128, cols]), "xst%d" % r, 16)
                else:
                    INC(None, "xst%d" % r, 16)
                ev = U4[:, r * 1024:(r + 1) * 1024]
                if on("vector"):
                    self.wge("mm2", base_mm2 + (idx + 1))
                    self.wge("xst%d" % r, base_xst[r] + 16 * (idx // 2 + 1))
                    if idx >= 2:
                        self.wge("out%d" % r, base_out[r] + 16 * (idx // 2))
                    INC(eng.scalar_tensor_tensor(ev, region, self.BIAS[:, d:d + 1], xs,
                                                 ALU.add, ALU.add), "evd", 1)
                else:
                    INC(None, "evd", 1)
                if on("sync"):
                    self.wge("evd", base_evd + (idx + 1))
                    INC(eng.dma_start(self.out_fm[d * 128:(d + 1) * 128, cols], ev), "out%d" % r, 16)
                else:
                    INC(None, "out%d" % r, 16)
                idx += 1


def _consts():
    maskc = np.zeros((128, 4, 512), dtype=np.float32)
    for i in range(4):
        k = np.arange(128)[:, None] + i * 128
        q = np.arange(512)[None, :]
        maskc[:, i, :] = np.where(k > q, NEG, 0.0).astype(np.float32)
    emat = np.zeros((16, 256), dtype=np.float32)
    for p in range(128):
        emat[p // 16, p] = 1.0
        emat[8 + p // 16, 128 + p] = 1.0
    identc = np.eye(128, dtype=np.float32)
    return maskc, emat, identc


def kernel(L0_attn_x, L0_attn_Wq, L0_attn_Wk, L0_attn_Wv, L0_attn_Wo,
           L0_attn_cos, L0_attn_sin, L0_ffn_W, L0_ffn_bias, L0_ffn_A,
           L0_ffn_Bf, L0_ffn_coeffs, L0_ffn_scales):
    x = np.asarray(L0_attn_x, dtype=np.float32)
    wq = np.ascontiguousarray(np.asarray(L0_attn_Wq, dtype=np.float32))
    wk = np.ascontiguousarray(np.asarray(L0_attn_Wk, dtype=np.float32))
    wv = np.ascontiguousarray(np.asarray(L0_attn_Wv, dtype=np.float32))
    wo = np.ascontiguousarray(np.asarray(L0_attn_Wo, dtype=np.float32))
    wf = np.ascontiguousarray(np.asarray(L0_ffn_W, dtype=np.float32).T)
    cosT = np.ascontiguousarray(np.asarray(L0_attn_cos, dtype=np.float32)[:S].T)
    sinT = np.ascontiguousarray(np.asarray(L0_attn_sin, dtype=np.float32)[:S].T)
    a_t = np.ascontiguousarray(np.asarray(L0_ffn_A, dtype=np.float32).reshape(256, D).T)
    bf_r = np.ascontiguousarray(np.asarray(L0_ffn_Bf, dtype=np.float32).transpose(0, 2, 1).reshape(256, D))
    bias2d = np.ascontiguousarray(np.asarray(L0_ffn_bias, dtype=np.float32).reshape(32, 128).T)
    coef = np.ascontiguousarray(np.asarray(L0_ffn_coeffs, dtype=np.float32).reshape(16, 1))
    scal = np.ascontiguousarray(np.asarray(L0_ffn_scales, dtype=np.float32).reshape(16, 1))
    maskc, emat, identc = _consts()

    common = dict(wq=wq, wk=wk, wv=wv, wo=wo, wf=wf, cosT=cosT, sinT=sinT,
                  a_t=a_t, bf_r=bf_r, bias2d=bias2d, coef=coef, scal=scal,
                  emat=emat, identc=identc, maskc=maskc)
    in_maps = []
    for c in range(8):
        m = dict(common)
        m["x_fm"] = np.ascontiguousarray(x[c].T)
        in_maps.append(m)

    b = B(upto=6, debug=False)
    res = run_bass_kernel_spmd(b.nc, in_maps, core_ids=list(range(8)))
    out = np.stack([np.asarray(res.results[c]["out_fm"]).T for c in range(8)])
    return np.ascontiguousarray(out)


# version 20
# speedup vs baseline: 1.0443x; 1.0443x over previous
import math
from contextlib import ExitStack

import numpy as np
import ml_dtypes

import concourse.bass as bass
import concourse.mybir as mybir
from concourse.bass_utils import run_bass_kernel_spmd

F32 = mybir.dt.float32
BF16 = mybir.dt.bfloat16
AF = mybir.ActivationFunctionType
ALU = mybir.AluOpType
AX = mybir.AxisListType

S = 2048
D = 4096
HALF = 1024
NH = 32
NKV = 8
HD = 128
NKC = 32
RSQ = 1.0 / math.sqrt(HD)
EPS = 1e-5
NEG = -1.0e9

SEM_NAMES = (
    "c0", "c0c", "ph",
    "p1x0", "p1x1", "p1b", "p1bf", "p1v", "p1r", "p1bc", "p1ev", "p1n",
    "wld0", "wld1", "mm2", "ev2", "rop", "qstg", "qsp", "sqs", "nmm", "nrm",
    "vtr", "vev", "bnd",
    "q0", "q1", "sc", "msk", "exp", "vdn", "rc", "rb", "rbe", "cev", "csp",
    "crl", "xst0", "xst1", "evd", "ev40", "ev41", "out0", "out1", "dbg",
    "bft0", "bft1", "xaev", "chn",
)

ENGS = ("sync", "gpsimd", "tensor", "vector", "scalar")


class B:
    def __init__(self, upto=6, debug=False):
        self.upto = upto
        self.debug = debug
        nc = bass.Bass("TRN2", target_bir_lowering=False)
        self.nc = nc

        ap = lambda t: t.ap()
        self.x_fm = ap(nc.dram_tensor("x_fm", [D, S], F32, kind="ExternalInput"))
        self.wq = ap(nc.dram_tensor("wq", [D, D], F32, kind="ExternalInput"))
        self.wk = ap(nc.dram_tensor("wk", [D, 1024], F32, kind="ExternalInput"))
        self.wv = ap(nc.dram_tensor("wv", [D, 1024], F32, kind="ExternalInput"))
        self.wo = ap(nc.dram_tensor("wo", [D, D], F32, kind="ExternalInput"))
        self.wf = ap(nc.dram_tensor("wf", [D, D], F32, kind="ExternalInput"))
        self.cosT = ap(nc.dram_tensor("cosT", [HD, S], F32, kind="ExternalInput"))
        self.sinT = ap(nc.dram_tensor("sinT", [HD, S], F32, kind="ExternalInput"))
        self.a_t = ap(nc.dram_tensor("a_t", [D, 256], F32, kind="ExternalInput"))
        self.bf_r = ap(nc.dram_tensor("bf_r", [256, D], F32, kind="ExternalInput"))
        self.bias2d = ap(nc.dram_tensor("bias2d", [128, 32], F32, kind="ExternalInput"))
        self.coef = ap(nc.dram_tensor("coef", [16, 1], F32, kind="ExternalInput"))
        self.scal = ap(nc.dram_tensor("scal", [16, 1], F32, kind="ExternalInput"))
        self.emat = ap(nc.dram_tensor("emat", [16, 256], F32, kind="ExternalInput"))
        self.identc = ap(nc.dram_tensor("identc", [128, 128], F32, kind="ExternalInput"))
        self.maskc = ap(nc.dram_tensor("maskc", [128, 4, 512], F32, kind="ExternalInput"))

        dk = "ExternalOutput" if debug else "Internal"
        self.q_spill = ap(nc.dram_tensor("q_spill", [NH, 128, S], BF16, kind=dk))
        self.ctx0_sp = ap(nc.dram_tensor("ctx0_sp", [NH, 128, HALF], BF16, kind=dk))
        self.xprime = ap(nc.dram_tensor("xprime", [D, S], F32, kind=dk))
        self.out_fm = ap(nc.dram_tensor("out_fm", [D, S], F32, kind="ExternalOutput"))
        if debug:
            self.dbg_ln1 = ap(nc.dram_tensor("dbg_ln1", [D, S], BF16, kind="ExternalOutput"))
            self.dbg_k = ap(nc.dram_tensor("dbg_k", [NKV, 128, S], BF16, kind="ExternalOutput"))
            self.dbg_v = ap(nc.dram_tensor("dbg_v", [NKV, 16, 128, 128], BF16, kind="ExternalOutput"))
            self.dbg_ctx1 = ap(nc.dram_tensor("dbg_ctx1", [NH, 128, HALF], BF16, kind="ExternalOutput"))
            self.dbg_negb = ap(nc.dram_tensor("dbg_negb", [128, 32], F32, kind="ExternalOutput"))
            self.dbg_ln2 = ap(nc.dram_tensor("dbg_ln2", [D, S], BF16, kind="ExternalOutput"))

        st = ExitStack()
        E = st.enter_context
        self.U1 = E(nc.sbuf_tensor("U1", [128, 32768], BF16))
        self.U2 = E(nc.sbuf_tensor("U2", [128, 32768], BF16))
        self.WS = E(nc.sbuf_tensor("WS", [128, 2, NKC, 128], BF16))
        self.U4 = E(nc.sbuf_tensor("U4", [128, 4096], F32))
        self.MR = E(nc.sbuf_tensor("MR", [128, 2048], F32))
        self.XST = E(nc.sbuf_tensor("XST", [128, 2048], F32))
        self.SCB = E(nc.sbuf_tensor("SCB", [128, 2048], BF16))
        self.OSTG = E(nc.sbuf_tensor("OSTG", [128, 1024], BF16))
        self.SQS = E(nc.sbuf_tensor("SQS", [128, 1024], BF16))
        self.MSK = E(nc.sbuf_tensor("MSK", [128, 4, 512], BF16))
        self.Q3 = E(nc.sbuf_tensor("Q3", [128, 4, 512], BF16))
        self.BFT = E(nc.sbuf_tensor("BFT", [128, 2, 2, 128], BF16))
        self.ROWS = E(nc.sbuf_tensor("ROWS", [1, 2048], F32))
        self.BND = E(nc.sbuf_tensor("BND", [1, 256], F32))
        self.NEGB = E(nc.sbuf_tensor("NEGB", [128, 32], F32))
        self.RBS = E(nc.sbuf_tensor("RBS", [128, 2, 512], BF16))
        self.ONESF = E(nc.sbuf_tensor("ONESF", [1, 128], F32))
        self.ONESB = E(nc.sbuf_tensor("ONESB", [128, 1], BF16))
        self.ONES2 = E(nc.sbuf_tensor("ONES2", [128, 128], BF16))
        self.ONESR = E(nc.sbuf_tensor("ONESR", [1, 128], BF16))
        self.IDF = E(nc.sbuf_tensor("IDF", [128, 128], F32))
        self.WEXP = E(nc.sbuf_tensor("WEXP", [128, 2], F32))
        self.BIAS = E(nc.sbuf_tensor("BIAS", [128, 32], F32))
        self.CS16 = E(nc.sbuf_tensor("CS16", [16, 2], F32))
        self.EM = E(nc.sbuf_tensor("EM", [16, 256], F32))
        self.WROW = E(nc.sbuf_tensor("WROW", [16, 1], F32))
        self.PS = E(nc.psum_tensor("PS", [128, 4096], F32))

        self.sems = {n: E(nc.semaphore("s_" + n)) for n in SEM_NAMES}
        self.block = E(nc.Block())
        self.block.sync(lambda e: self.emit("sync", e))
        self.block.gpsimd(lambda e: self.emit("gpsimd", e))
        self.block.tensor(lambda e: self.emit("tensor", e))
        self.block.vector(lambda e: self.emit("vector", e))
        self.block.scalar(lambda e: self.emit("scalar", e))
        st.close()

    def on(self, name):
        return self.me == name

    def INC(self, inst, sem, n=1):
        self.C[sem] += n
        if inst is not None:
            inst.then_inc(self.sems[sem], n)

    def wge(self, sem, val):
        self.eng.wait_ge(self.sems[sem], val)

    def W(self, engine, sem):
        if self.me == engine:
            self.eng.wait_ge(self.sems[sem], self.C[sem])

    def bar(self):
        for name in ENGS:
            self.C["ph"] += 1
            if self.on(name):
                self.eng.drain()
                self.eng.sem_inc(self.sems["ph"], 1)
        self.eng.wait_ge(self.sems["ph"], self.C["ph"])

    def arena(self, U, i):
        return U[:, i * 1024:(i + 1) * 1024]

    def kfm(self, kv):
        return self.U2[:, kv * 2048:(kv + 1) * 2048]

    def vtok(self, kv, tile):
        off = 16384 + (kv * 16 + tile) * 128
        return self.U2[:, off:off + 128]

    def emit(self, me, eng):
        self.me = me
        self.eng = eng
        self.C = {k: 0 for k in SEM_NAMES}

        self.p0()
        self.bar()
        for H in (0, 1):
            self.p_ln(self.x_fm, lambda kc: self.arena(self.U1, kc), H,
                      dbg=self.dbg_ln1 if self.debug else None)
            if self.upto >= 2:
                self.p2(H)
            if self.upto >= 3:
                self.p3(H)
                self.bar()
            else:
                self.bar()
        if self.upto >= 2 and self.debug:
            self.p2_dbg()
            self.bar()
        if self.upto >= 4:
            self.p4()
            self.bar()
        if self.upto >= 5:
            for H in (0, 1):
                U = self.U1 if H == 0 else self.U2
                self.p_ln(self.xprime, lambda kc, U=U: self.arena(U, kc), H,
                          dbg=self.dbg_ln2 if self.debug else None)
        if self.upto >= 6:
            self.p6()
        self.bar()
        if self.on("sync"):
            self.eng.wait_ge(self.sems["out0"], self.C["out0"])
            self.eng.wait_ge(self.sems["out1"], self.C["out1"])
            self.eng.wait_ge(self.sems["dbg"], self.C["dbg"])

    def p0(self):
        on, INC = self.on, self.INC
        eng = self.eng
        if on("gpsimd"):
            INC(eng.dma_start(self.U4[:, 0:2048], self.cosT[:, :]), "c0", 16)
            INC(eng.dma_start(self.U4[:, 2048:4096], self.sinT[:, :]), "c0", 16)
            INC(eng.dma_start(self.MSK[:, :, :], self.maskc), "c0", 16)
            INC(eng.dma_start(self.IDF[:, :], self.identc), "c0", 16)
            INC(eng.dma_start(self.CS16[:, 0:1], self.coef), "c0", 16)
            INC(eng.dma_start(self.CS16[:, 1:2], self.scal), "c0", 16)
            INC(eng.dma_start(self.EM[:, :], self.emat), "c0", 16)
            INC(eng.dma_start(self.BIAS[:, :], self.bias2d), "c0", 16)
        else:
            for _ in range(8):
                INC(None, "c0", 16)
        if on("vector"):
            INC(eng.memset(self.ONESF[:, :], 1.0), "c0c", 1)
            INC(eng.memset(self.ONESB[:, :], 1.0), "c0c", 1)
            INC(eng.memset(self.ONES2[:, :], 1.0), "c0c", 1)
            INC(eng.memset(self.ONESR[:, :], 1.0), "c0c", 1)
            self.wge("c0", self.C["c0"])
            INC(eng.tensor_mul(self.WROW[:, :], self.CS16[:, 0:1], self.CS16[:, 1:2]), "c0c", 1)
        else:
            for _ in range(5):
                INC(None, "c0c", 1)
        if self.on("tensor"):
            self.wge("c0", self.C["c0"])
            self.wge("c0c", self.C["c0c"])
        for ec in range(2):
            if on("tensor"):
                mm = eng.matmul(self.PS[:, 3072 + ec:3073 + ec],
                                self.EM[:, ec * 128:(ec + 1) * 128],
                                self.WROW[:, :], start=True, stop=True)
                INC(mm, "c0c", 1)
            else:
                INC(None, "c0c", 1)
        self.W("scalar", "c0c")
        if on("scalar"):
            INC(eng.copy(self.WEXP[:, 0:2], self.PS[:, 3072:3074]), "c0c", 1)
        else:
            INC(None, "c0c", 1)

    def p_ln(self, src, dstf, H, dbg=None):
        on, INC = self.on, self.INC
        eng = self.eng
        PS, XST, OSTG, SQS = self.PS, self.XST, self.OSTG, self.SQS
        ROWS, MR, ONESB, ONESF = self.ROWS, self.MR, self.ONESB, self.ONESF
        cols = slice(H * HALF, (H + 1) * HALF)

        xslot = lambda kc: XST[:, (kc % 2) * 1024:(kc % 2 + 1) * 1024]
        sqslot = lambda kc: OSTG[:, :] if kc % 2 == 0 else SQS[:, :]

        base_x = [self.C["p1x0"], self.C["p1x1"]]
        base_b = self.C["p1b"]
        base_bf = self.C["p1bf"]

        for kc in range(NKC):
            sx = "p1x%d" % (kc % 2)
            if on("sync"):
                if kc < 2:
                    self.wge("p1b", base_b)
                    self.wge("p1bc", self.C["p1bc"])
                else:
                    self.wge("p1b", base_b + (kc - 1))
                INC(eng.dma_start(xslot(kc), src[kc * 128:(kc + 1) * 128, cols]), sx, 16)
            else:
                INC(None, sx, 16)
            if on("scalar"):
                self.wge(sx, base_x[kc % 2] + 16 * (kc // 2 + 1))
                if kc == 0:
                    self.wge("p1bf", self.C["p1bf"])
                if kc >= 2:
                    self.wge("p1bf", base_bf + (kc - 1))
                eng.copy(dstf(kc), xslot(kc))
                INC(eng.activation(sqslot(kc), xslot(kc), AF.Square), "p1b", 1)
            else:
                INC(None, "p1b", 1)
            if on("tensor"):
                self.wge("p1b", base_b + (kc + 1))
                if kc == 0:
                    self.wge("p1ev", self.C["p1ev"])
                st, sp = (kc == 0), (kc == NKC - 1)
                xb = dstf(kc)
                eng.matmul(PS[0:1, 0:512], ONESB[:, :], xb[:, 0:512], start=st, stop=sp)
                eng.matmul(PS[0:1, 512:1024], ONESB[:, :], xb[:, 512:1024], start=st, stop=sp)
                eng.matmul(PS[0:1, 1024:1536], ONESB[:, :], sqslot(kc)[:, 0:512], start=st, stop=sp)
                i4 = eng.matmul(PS[0:1, 1536:2048], ONESB[:, :], sqslot(kc)[:, 512:1024],
                                start=st, stop=sp)
                INC(i4, "p1bf", 1)
            else:
                INC(None, "p1bf", 1)

        mrow = ROWS[0:1, 0:1024]
        varr = ROWS[0:1, 1024:2048]
        tmp = XST[0:1, 0:1024]
        rrow = XST[0:1, 1024:2048]
        if on("vector"):
            self.wge("p1bf", base_bf + NKC)
            eng.tensor_scalar_mul(mrow, PS[0:1, 0:1024], 1.0 / D)
            INC(eng.tensor_scalar_mul(varr, PS[0:1, 1024:2048], 1.0 / D), "chn", 1)
            self.wge("chn", self.C["chn"])
            INC(eng.tensor_mul(tmp, mrow, mrow), "chn", 1)
            self.wge("chn", self.C["chn"])
            INC(eng.tensor_sub(varr, varr, tmp), "chn", 1)
            self.wge("chn", self.C["chn"])
            INC(eng.tensor_scalar_add(varr, varr, EPS), "p1v", 1)
        else:
            INC(None, "chn", 1)
            INC(None, "chn", 1)
            INC(None, "chn", 1)
            INC(None, "p1v", 1)
        self.W("scalar", "p1v")
        if on("scalar"):
            INC(eng.activation(varr, varr, AF.Sqrt), "p1r", 1)
        else:
            INC(None, "p1r", 1)
        self.W("vector", "p1r")
        if on("vector"):
            INC(eng.reciprocal(rrow, varr), "p1v", 1)
        else:
            INC(None, "p1v", 1)
        self.W("tensor", "p1v")
        if on("tensor"):
            eng.matmul(PS[:, 2048:2560], ONESF[0:1, :], mrow[:, 0:512], start=True, stop=True)
            eng.matmul(PS[:, 2560:3072], ONESF[0:1, :], mrow[:, 512:1024], start=True, stop=True)
            eng.matmul(PS[:, 0:512], ONESF[0:1, :], rrow[:, 0:512], start=True, stop=True)
            INC(eng.matmul(PS[:, 512:1024], ONESF[0:1, :], rrow[:, 512:1024],
                           start=True, stop=True), "p1bc", 1)
        else:
            INC(None, "p1bc", 1)
        self.W("scalar", "p1bc")
        if on("scalar"):
            eng.copy(MR[:, 0:1024], PS[:, 2048:3072])
            INC(eng.copy(MR[:, 1024:2048], PS[:, 0:1024]), "p1ev", 1)
        else:
            INC(None, "p1ev", 1)

        base_n = self.C["p1n"]
        if on("vector"):
            self.wge("p1ev", self.C["p1ev"])
            i = None
            for kc in range(NKC):
                i = eng.tensor_sub(dstf(kc), dstf(kc), MR[:, 0:1024])
            INC(i, "chn", 1)
            self.wge("chn", self.C["chn"])
            for kc in range(NKC):
                INC(eng.tensor_mul(dstf(kc), dstf(kc), MR[:, 1024:2048]), "p1n", 1)
        else:
            INC(None, "chn", 1)
            for kc in range(NKC):
                INC(None, "p1n", 1)
        if dbg is not None:
            for kc in range(NKC):
                if on("sync"):
                    self.wge("p1n", base_n + (kc + 1))
                    INC(eng.dma_start(dbg[kc * 128:(kc + 1) * 128, cols], dstf(kc)), "dbg", 16)
                else:
                    INC(None, "dbg", 16)
            if on("sync"):
                self.wge("dbg", self.C["dbg"])

    def p2(self, H):
        on, INC = self.on, self.INC
        eng = self.eng
        PS, WS, MR, XST, OSTG, SQS = self.PS, self.WS, self.MR, self.XST, self.OSTG, self.SQS
        cosv = self.U4[:, 0:2048]
        sinv = self.U4[:, 2048:4096]
        cols = slice(H * HALF, (H + 1) * HALF)

        base_wld = [self.C["wld0"], self.C["wld1"]]
        base_mm2 = self.C["mm2"]
        base_ev2 = self.C["ev2"]
        base_vtr = self.C["vtr"]
        base_vev = self.C["vev"]

        def wsrc(d):
            if d < 32:
                return self.wq[:, d * 128:(d + 1) * 128]
            if d < 40:
                return self.wk[:, (d - 32) * 128:(d - 31) * 128]
            return self.wv[:, (d - 40) * 128:(d - 39) * 128]

        if on("gpsimd"):
            for d in range(48):
                if d >= 2:
                    self.wge("mm2", base_mm2 + (d - 1))
                elif d == 0:
                    self.wge("mm2", base_mm2)
                src = wsrc(d).rearrange("(k p) c -> p k c", p=128)
                INC(eng.dma_start(self.WS[:, d % 2, :, :], src), "wld%d" % (d % 2), 16)
        else:
            for d in range(48):
                INC(None, "wld%d" % (d % 2), 16)

        vcount = 0
        qkn = 0
        pending_norm = []

        def sqs_slot(n):
            return self.SCB[:, (n % 2) * 1024:(n % 2 + 1) * 1024]

        def emit_norm(slot, n):
            sq = sqs_slot(n)
            if on("tensor"):
                self.wge("sqs", self.C["sqs"])
                self.wge("nrm", self.C["nrm"])
                eng.matmul(PS[0:1, 2048:2560], self.ONESB[:, :], sq[:, 0:512],
                           start=True, stop=True)
                INC(eng.matmul(PS[0:1, 2560:3072], self.ONESB[:, :], sq[:, 512:1024],
                               start=True, stop=True), "nmm", 1)
            else:
                INC(None, "nmm", 1)
            if on("vector"):
                self.wge("nmm", self.C["nmm"])
                INC(eng.tensor_reduce(self.BND[0:1, slot:slot + 1], PS[0:1, 2048:3072],
                                      AX.X, ALU.max), "nrm", 1)
            else:
                INC(None, "nrm", 1)

        for d in range(48):
            r = d % 2
            region = PS[:, r * 1024:(r + 1) * 1024]
            kind = "q" if d < 32 else ("k" if d < 40 else "v")
            h = d if d < 32 else (d - 32 if d < 40 else d - 40)

            if on("tensor"):
                self.wge("wld%d" % (d % 2), base_wld[d % 2] + 16 * (d // 2 + 1))
                if d == 0:
                    self.wge("p1n", self.C["p1n"])
                    self.wge("p1ev", self.C["p1ev"])
                if d >= 2:
                    self.wge("ev2", base_ev2 + (d - 1))
                for kc in range(NKC):
                    st, sp = (kc == 0), (kc == NKC - 1)
                    lhsT = WS[:, d % 2, kc, :]
                    ln = self.arena(self.U1, kc)
                    eng.matmul(region[:, 0:512], lhsT, ln[:, 0:512], start=st, stop=sp)
                    mmi = eng.matmul(region[:, 512:1024], lhsT, ln[:, 512:1024], start=st, stop=sp)
                INC(mmi, "mm2", 1)
            else:
                INC(None, "mm2", 1)
            if pending_norm:
                emit_norm(*pending_norm.pop(0))

            if kind in ("q", "k"):
                t1 = MR[:, 0:1024]
                t2 = MR[:, 1024:2048]
                u = XST[:, 0:1024]
                if on("vector"):
                    self.wge("mm2", base_mm2 + (d + 1))
                    self.wge("qstg", self.C["qstg"])
                    cv = cosv[:, cols]
                    sv = sinv[:, cols]
                    eng.tensor_mul(t1, region, cv)
                    eng.tensor_mul(t2[0:64, :], region[64:128, :], sv[0:64, :])
                    INC(eng.tensor_mul(t2[64:128, :], region[0:64, :], sv[64:128, :]), "ev2", 1)
                    self.wge("ev2", self.C["ev2"])
                    INC(eng.tensor_sub(u[0:64, :], t1[0:64, :], t2[0:64, :]), "chn", 1)
                    self.wge("chn", self.C["chn"])
                    INC(eng.tensor_add(u[64:128, :], t1[64:128, :], t2[64:128, :]), "rop", 1)
                else:
                    INC(None, "ev2", 1)
                    INC(None, "chn", 1)
                    INC(None, "rop", 1)
                if kind == "q":
                    if on("scalar"):
                        if d == 0:
                            self.wge("p1bf", self.C["p1bf"])
                            self.wge("rb", self.C["rb"])
                        self.wge("rop", self.C["rop"])
                        self.wge("qsp", self.C["qsp"])
                        if qkn >= 2:
                            self.wge("nmm", self.C["nmm"] - 1)
                        INC(eng.activation(OSTG[:, :], u, AF.Copy, scale=RSQ), "qstg", 1)
                        self.wge("qstg", self.C["qstg"])
                        INC(eng.activation(sqs_slot(qkn), OSTG[:, :], AF.Square), "sqs", 1)
                    else:
                        INC(None, "qstg", 1)
                        INC(None, "sqs", 1)
                    if on("sync"):
                        self.wge("qstg", self.C["qstg"])
                        INC(eng.dma_start(self.q_spill[h, :, cols], OSTG[:, :]), "qsp", 16)
                    else:
                        INC(None, "qsp", 16)
                else:
                    if on("scalar"):
                        self.wge("rop", self.C["rop"])
                        if qkn >= 2:
                            self.wge("nmm", self.C["nmm"] - 1)
                        INC(eng.activation(self.kfm(h)[:, cols], u, AF.Copy), "qstg", 1)
                        self.wge("qstg", self.C["qstg"])
                        INC(eng.activation(sqs_slot(qkn), self.kfm(h)[:, cols], AF.Square),
                            "sqs", 1)
                    else:
                        INC(None, "qstg", 1)
                        INC(None, "sqs", 1)
                slot = (h * 2 + H) if kind == "q" else (64 + h * 2 + H)
                pending_norm.append((slot, qkn))
                qkn += 1
            else:
                if on("scalar"):
                    self.wge("mm2", base_mm2 + (d + 1))
                    self.wge("qstg", self.C["qstg"])
                    INC(eng.copy(XST[:, 0:1024], region), "ev2", 1)
                else:
                    INC(None, "ev2", 1)
                for tt in range(8):
                    trb = self.PS[:, 2048 + (tt % 2) * 512:2048 + (tt % 2) * 512 + 128]
                    if on("tensor"):
                        self.wge("ev2", self.C["ev2"])
                        if vcount == 0 and tt == 0:
                            self.wge("nrm", self.C["nrm"])
                        ti = vcount * 8 + tt
                        if ti >= 2:
                            self.wge("vev", base_vev + (ti - 1))
                        INC(eng.transpose(trb, XST[:, tt * 128:(tt + 1) * 128], self.IDF[:, :]),
                            "vtr", 1)
                    else:
                        INC(None, "vtr", 1)
                    if on("scalar"):
                        self.wge("vtr", self.C["vtr"])
                        INC(eng.copy(self.vtok(h, H * 8 + tt), trb), "vev", 1)
                    else:
                        INC(None, "vev", 1)
                vcount += 1
        while pending_norm:
            emit_norm(*pending_norm.pop(0))
        if on("sync"):
            self.wge("qsp", self.C["qsp"])

    def p2_dbg(self):
        on, INC = self.on, self.INC
        eng = self.eng
        if on("sync"):
            for kv in range(NKV):
                INC(eng.dma_start(self.dbg_k[kv], self.kfm(kv)), "dbg", 16)
                for t in range(16):
                    INC(eng.dma_start(self.dbg_v[kv, t], self.vtok(kv, t)), "dbg", 16)
            INC(eng.dma_start(self.dbg_negb, self.NEGB[:, :]), "dbg", 16)
            self.wge("dbg", self.C["dbg"])
        else:
            for _ in range(NKV * 17 + 1):
                INC(None, "dbg", 16)

    def p3(self, H):
        on, INC = self.on, self.INC
        eng = self.eng
        PS, SCB, Q3, MSK, RBS = self.PS, self.SCB, self.Q3, self.MSK, self.RBS
        BND, NEGB, ROWS = self.BND, self.NEGB, self.ROWS
        RCB = self.SQS

        self.W("vector", "nrm")
        if on("vector"):
            self.wge("qsp", self.C["qsp"])
            i = None
            for kv in range(NKV):
                kslot = 64 + kv * 2 + H
                if H == 0:
                    i = eng.tensor_copy(BND[0:1, 80 + kv:81 + kv], BND[0:1, kslot:kslot + 1])
                else:
                    i = eng.tensor_max(BND[0:1, 80 + kv:81 + kv], BND[0:1, 80 + kv:81 + kv],
                                       BND[0:1, kslot:kslot + 1])
            INC(i, "chn", 1)
            self.wge("chn", self.C["chn"])
            for h in range(NH):
                qslot = h * 2 + H
                i = eng.tensor_mul(BND[0:1, 96 + h:97 + h], BND[0:1, qslot:qslot + 1],
                                   BND[0:1, 80 + h // 4:81 + h // 4])
            INC(i, "chn", 1)
            self.wge("chn", self.C["chn"])
            INC(eng.tensor_reduce(BND[0:1, 192:193], BND[0:1, 96:128], AX.X, ALU.max),
                "bnd", 1)
        else:
            INC(None, "chn", 1)
            INC(None, "chn", 1)
            INC(None, "bnd", 1)
        self.W("scalar", "bnd")
        if on("scalar"):
            INC(eng.activation(BND[0:1, 193:194], BND[0:1, 192:193], AF.Sqrt), "chn", 1)
            self.wge("chn", self.C["chn"])
            INC(eng.mul(BND[0:1, 194:195], BND[0:1, 193:194], -1.0), "bnd", 1)
        else:
            INC(None, "chn", 1)
            INC(None, "bnd", 1)
        self.W("tensor", "bnd")
        if on("tensor"):
            INC(eng.matmul(PS[:, 3072:3073], self.ONESF[0:1, :], BND[0:1, 194:195],
                           start=True, stop=True), "bnd", 1)
        else:
            INC(None, "bnd", 1)
        self.W("scalar", "bnd")
        if on("scalar"):
            INC(eng.copy(NEGB[:, 0:1], PS[:, 3072:3073]), "bnd", 1)
        else:
            INC(None, "bnd", 1)
        self.W("tensor", "bnd")
        self.W("scalar", "bnd")

        base_q = [self.C["q0"], self.C["q1"]]
        base_cev = self.C["cev"]
        base_rc = self.C["rc"]
        base_rb = self.C["rb"]
        base_rbe = self.C["rbe"]
        base_csp = self.C["csp"]
        sc_tot = []
        pends = []

        def rcb_row(g, head):
            t = self.SQS if g % 2 == 0 else self.OSTG
            return t[0:1, head * 512:(head + 1) * 512]

        def emit_tail_pe(g):
            gi, h0, h1, jj = g
            if on("tensor"):
                self.wge("rc", base_rc + 2 * (gi + 1))
                self.wge("cev", base_cev + min(2 * (gi + 2), 2 * NH))
                if gi >= 1:
                    self.wge("rbe", base_rbe + 2 * gi)
                INC(eng.matmul(PS[:, 2048:2560], self.ONESR[0:1, :], rcb_row(gi, 0),
                               start=True, stop=True), "rb", 1)
                INC(eng.matmul(PS[:, 2560:3072], self.ONESR[0:1, :], rcb_row(gi, 1),
                               start=True, stop=True), "rb", 1)
            else:
                INC(None, "rb", 1)
                INC(None, "rb", 1)

        def emit_tail_dve(g):
            gi, h0, h1, jj = g
            d0 = self.arena(self.U1, h0)[:, jj * 512:(jj + 1) * 512]
            d1 = self.arena(self.U1, h1)[:, jj * 512:(jj + 1) * 512]
            if on("vector"):
                self.wge("rb", base_rb + 2 * (gi + 1))
                self.wge("nmm", self.C["nmm"])
                INC(eng.tensor_copy(RBS[:, 0, :], PS[:, 2048:2560]), "rbe", 1)
                INC(eng.tensor_copy(RBS[:, 1, :], PS[:, 2560:3072]), "rbe", 1)
                self.wge("rbe", self.C["rbe"])
                self.wge("cev", base_cev + 2 * (gi + 1))
                eng.tensor_mul(d0, d0, RBS[:, 0, :])
                INC(eng.tensor_mul(d1, d1, RBS[:, 1, :]), "nmm", 1)
            else:
                INC(None, "rbe", 1)
                INC(None, "rbe", 1)
                INC(None, "nmm", 1)
            if H == 0:
                if on("sync"):
                    self.wge("nmm", self.C["nmm"])
                    INC(eng.dma_start(self.ctx0_sp[h0, :, jj * 512:(jj + 1) * 512], d0),
                        "csp", 16)
                    INC(eng.dma_start(self.ctx0_sp[h1, :, jj * 512:(jj + 1) * 512], d1),
                        "csp", 16)
                else:
                    INC(None, "csp", 16)
                    INC(None, "csp", 16)

        idx = 0
        for hp in range(NH // 2):
            kv = hp // 2
            h0, h1 = 2 * hp, 2 * hp + 1
            for jj in range(2):
                q0 = H * 1024 + jj * 512
                kcn = q0 // 128 + 4
                p = idx % 2
                qs0 = Q3[:, p * 2, :]
                qs1 = Q3[:, p * 2 + 1, :]
                sq = "q%d" % p

                if on("gpsimd"):
                    if idx >= 2:
                        self.wge("sc", sc_tot[idx - 2])
                    else:
                        self.wge("qsp", self.C["qsp"] - 16 * 30)
                    INC(eng.dma_start(qs0, self.q_spill[h0, :, q0:q0 + 512]), sq, 16)
                    INC(eng.dma_start(qs1, self.q_spill[h1, :, q0:q0 + 512]), sq, 16)
                else:
                    INC(None, sq, 16)
                    INC(None, sq, 16)

                base_sc = self.C["sc"]
                base_exp = self.C["exp"]
                base_vdn = self.C["vdn"]
                diag0 = q0 // 128

                def region(kc):
                    return PS[:, (kc % 2) * 1024:(kc % 2 + 1) * 1024]

                def emit_scores(kc):
                    if on("tensor"):
                        if kc == 0:
                            self.wge(sq, base_q[p] + 32 * (idx // 2 + 1))
                            if idx == 0:
                                self.wge("ev2", self.C["ev2"])
                        if kc >= 2:
                            self.wge("exp", base_exp + (kc - 1))
                        lhsT = self.kfm(kv)[:, kc * 128:(kc + 1) * 128]
                        INC(eng.matmul(region(kc)[:, 0:512], lhsT, qs0,
                                       start=True, stop=True), "sc", 1)
                        INC(eng.matmul(region(kc)[:, 512:1024], lhsT, qs1,
                                       start=True, stop=True), "sc", 1)
                    else:
                        INC(None, "sc", 1)
                        INC(None, "sc", 1)

                emit_scores(0)
                if kcn > 1:
                    emit_scores(1)
                if len(pends) == 2:
                    g = pends.pop(0)
                    emit_tail_pe(g)
                    emit_tail_dve(g)
                for kc in range(kcn):
                    isdiag = kc >= diag0
                    if isdiag:
                        if on("vector"):
                            self.wge("sc", base_sc + 2 * (kc + 1))
                            mi = kc - diag0
                            eng.tensor_add(region(kc)[:, 0:512], region(kc)[:, 0:512],
                                           MSK[:, mi, :])
                            INC(eng.tensor_add(region(kc)[:, 512:1024],
                                               region(kc)[:, 512:1024], MSK[:, mi, :]),
                                "msk", 1)
                        else:
                            INC(None, "msk", 1)
                    if on("scalar"):
                        if isdiag:
                            self.wge("msk", self.C["msk"])
                        else:
                            self.wge("sc", base_sc + 2 * (kc + 1))
                        if kc >= 2:
                            self.wge("vdn", base_vdn + 2 * (kc - 1))
                        INC(eng.activation(SCB[:, (kc % 2) * 1024:(kc % 2 + 1) * 1024],
                                           region(kc), AF.Exp,
                                           bias=NEGB[:, 0:1], scale=1.0), "exp", 1)
                    else:
                        INC(None, "exp", 1)
                    if on("tensor"):
                        self.wge("exp", base_exp + (kc + 1))
                        if kc == 0:
                            self.wge("rc", base_rc + 2 * idx)
                            self.wge("cev", base_cev + 2 * idx)
                            if idx >= 1:
                                self.wge("rbe", base_rbe + 2 * (idx - 1))
                            else:
                                self.wge("vev", self.C["vev"])
                                self.wge("nrm", self.C["nrm"])
                        st, sp = (kc == 0), (kc == kcn - 1)
                        e0 = SCB[:, (kc % 2) * 1024:(kc % 2) * 1024 + 512]
                        e1 = SCB[:, (kc % 2) * 1024 + 512:(kc % 2 + 1) * 1024]
                        eng.matmul(PS[:, 3072:3584], self.ONES2[:, :], e0, start=st, stop=sp)
                        eng.matmul(PS[:, 3584:4096], self.ONES2[:, :], e1, start=st, stop=sp)
                        vt = self.vtok(kv, kc)
                        INC(eng.matmul(PS[:, 2048:2560], vt, e0, start=st, stop=sp), "vdn", 1)
                        INC(eng.matmul(PS[:, 2560:3072], vt, e1, start=st, stop=sp), "vdn", 1)
                    else:
                        INC(None, "vdn", 1)
                        INC(None, "vdn", 1)
                    if kc + 2 < kcn:
                        emit_scores(kc + 2)
                sc_tot.append(self.C["sc"])

                d0 = self.arena(self.U1, h0)[:, jj * 512:(jj + 1) * 512]
                d1 = self.arena(self.U1, h1)[:, jj * 512:(jj + 1) * 512]
                if on("scalar"):
                    self.wge("vdn", base_vdn + 2 * kcn)
                    INC(eng.copy(d0, PS[:, 2048:2560]), "cev", 1)
                    INC(eng.copy(d1, PS[:, 2560:3072]), "cev", 1)
                else:
                    INC(None, "cev", 1)
                    INC(None, "cev", 1)
                rc0 = ROWS[0:1, 0:512]
                rc1 = ROWS[0:1, 512:1024]
                if on("vector"):
                    self.wge("vdn", base_vdn + 2 * kcn)
                    INC(eng.reciprocal(rc0, PS[0:1, 3072:3584]), "chn", 1)
                    INC(eng.reciprocal(rc1, PS[0:1, 3584:4096]), "chn", 1)
                    self.wge("chn", self.C["chn"])
                    INC(eng.tensor_copy(rcb_row(idx, 0), rc0), "rc", 1)
                    INC(eng.tensor_copy(rcb_row(idx, 1), rc1), "rc", 1)
                else:
                    INC(None, "chn", 1)
                    INC(None, "chn", 1)
                    INC(None, "rc", 1)
                    INC(None, "rc", 1)
                pends.append((idx, h0, h1, jj))
                idx += 1
        for g in pends:
            emit_tail_pe(g)
            emit_tail_dve(g)
        pends = []

        if H == 1 and self.debug:
            if on("sync"):
                for h in range(NH):
                    self.wge("cev", self.C["cev"])
                    INC(eng.dma_start(self.dbg_ctx1[h], self.arena(self.U1, h)), "dbg", 16)
            else:
                for h in range(NH):
                    INC(None, "dbg", 16)
        if on("sync"):
            self.wge("csp", self.C["csp"])
            self.wge("dbg", self.C["dbg"])

    def p4(self):
        on, INC = self.on, self.INC
        eng = self.eng
        PS, WS, XST, U4 = self.PS, self.WS, self.XST, self.U4

        base_wld = [self.C["wld0"], self.C["wld1"]]
        base_mm2 = self.C["mm2"]
        base_evd = self.C["evd"]
        base_xst = [self.C["xst0"], self.C["xst1"]]
        base_ev4 = [self.C["ev40"], self.C["ev41"]]

        if on("gpsimd"):
            for h in range(NH):
                INC(eng.dma_start(self.arena(self.U2, h), self.ctx0_sp[h]), "crl", 16)
            for d in range(32):
                if d >= 2:
                    self.wge("mm2", base_mm2 + 2 * (d - 1))
                src = self.wo[:, d * 128:(d + 1) * 128].rearrange("(k p) c -> p k c", p=128)
                INC(eng.dma_start(WS[:, d % 2, :, :], src), "wld%d" % (d % 2), 16)
        else:
            for h in range(NH):
                INC(None, "crl", 16)
            for d in range(32):
                INC(None, "wld%d" % (d % 2), 16)

        idx = 0
        for d in range(32):
            for th in range(2):
                r = idx % 2
                region = PS[:, r * 1024:(r + 1) * 1024]
                cols = slice(th * HALF, (th + 1) * HALF)
                if on("tensor"):
                    self.wge("wld%d" % (d % 2), base_wld[d % 2] + 16 * (d // 2 + 1))
                    if idx == 0:
                        self.wge("crl", self.C["crl"])
                    if idx >= 2:
                        self.wge("evd", base_evd + (idx - 1))
                    for kc in range(NKC):
                        st, sp = (kc == 0), (kc == NKC - 1)
                        lhsT = WS[:, d % 2, kc, :]
                        rhs = self.arena(self.U2 if th == 0 else self.U1, kc)
                        eng.matmul(region[:, 0:512], lhsT, rhs[:, 0:512], start=st, stop=sp)
                        mmi = eng.matmul(region[:, 512:1024], lhsT, rhs[:, 512:1024],
                                         start=st, stop=sp)
                    INC(mmi, "mm2", 1)
                else:
                    INC(None, "mm2", 1)
                xs = XST[:, r * 1024:(r + 1) * 1024]
                if on("sync"):
                    if idx >= 2:
                        self.wge("evd", base_evd + (idx - 1))
                    INC(eng.dma_start(xs, self.x_fm[d * 128:(d + 1) * 128, cols]), "xst%d" % r, 16)
                else:
                    INC(None, "xst%d" % r, 16)
                ev = U4[:, r * 1024:(r + 1) * 1024]
                if on("vector"):
                    self.wge("mm2", base_mm2 + (idx + 1))
                    self.wge("xst%d" % r, base_xst[r] + 16 * (idx // 2 + 1))
                    if idx >= 2:
                        self.wge("ev4%d" % r, base_ev4[r] + 16 * (idx // 2))
                    INC(eng.tensor_add(ev, region, xs), "evd", 1)
                else:
                    INC(None, "evd", 1)
                if on("sync"):
                    self.wge("evd", base_evd + (idx + 1))
                    INC(eng.dma_start(self.xprime[d * 128:(d + 1) * 128, cols], ev), "ev4%d" % r, 16)
                else:
                    INC(None, "ev4%d" % r, 16)
                idx += 1
        if on("sync"):
            self.wge("ev40", self.C["ev40"])
            self.wge("ev41", self.C["ev41"])

    def p6(self):
        on, INC = self.on, self.INC
        eng = self.eng
        PS, WS, XST, U4, BFT = self.PS, self.WS, self.XST, self.U4, self.BFT

        if on("gpsimd"):
            self.wge("mm2", self.C["mm2"])
            for ec in range(2):
                src = self.a_t[:, ec * 128:(ec + 1) * 128].rearrange("(k p) e -> p k e", p=128)
                INC(eng.dma_start(WS[:, ec, :, :], src), "wld%d" % ec, 16)
        else:
            for ec in range(2):
                INC(None, "wld%d" % ec, 16)
        for ec in range(2):
            if on("tensor"):
                self.wge("wld0", self.C["wld0"])
                self.wge("wld1", self.C["wld1"])
                if ec == 0:
                    self.wge("p1n", self.C["p1n"])
                    self.wge("p1ev", self.C["p1ev"])
                if ec == 1:
                    self.wge("xaev", self.C["xaev"])
                for kc in range(NKC):
                    st, sp = (kc == 0), (kc == NKC - 1)
                    lhsT = WS[:, ec, kc, :]
                    for th in range(2):
                        rhs = self.arena(self.U1 if th == 0 else self.U2, kc)
                        eng.matmul(PS[:, th * 1024:th * 1024 + 512], lhsT, rhs[:, 0:512],
                                   start=st, stop=sp)
                        mmi = eng.matmul(PS[:, th * 1024 + 512:(th + 1) * 1024], lhsT,
                                         rhs[:, 512:1024], start=st, stop=sp)
                INC(mmi, "mm2", 1)
            else:
                INC(None, "mm2", 1)
            if on("vector"):
                self.wge("mm2", self.C["mm2"])
                xa_out = self.SCB[:, 0:2048] if ec == 0 else self.MSK[:, :, :]
                INC(eng.tensor_scalar(xa_out, PS[:, 0:2048], self.WEXP[:, ec:ec + 1],
                                      None, ALU.mult), "xaev", 1)
            else:
                INC(None, "xaev", 1)

        base_wld = [self.C["wld0"], self.C["wld1"]]
        base_mm2 = self.C["mm2"]
        base_bft = [self.C["bft0"], self.C["bft1"]]
        base_evd = self.C["evd"]
        base_xst = [self.C["xst0"], self.C["xst1"]]
        base_out = [self.C["out0"], self.C["out1"]]

        if on("gpsimd"):
            for d in range(32):
                if d < 2:
                    self.wge("xaev", self.C["xaev"])
                else:
                    self.wge("mm2", base_mm2 + 2 * (d - 1))
                src = self.wf[:, d * 128:(d + 1) * 128].rearrange("(k p) c -> p k c", p=128)
                INC(eng.dma_start(WS[:, d % 2, :, :], src), "wld%d" % (d % 2), 16)
                bsrc = self.bf_r[:, d * 128:(d + 1) * 128].rearrange("(e p) c -> p e c", p=128)
                INC(eng.dma_start(BFT[:, d % 2, :, :], bsrc), "bft%d" % (d % 2), 16)
        else:
            for d in range(32):
                INC(None, "wld%d" % (d % 2), 16)
                INC(None, "bft%d" % (d % 2), 16)

        idx = 0
        for d in range(32):
            for th in range(2):
                r = idx % 2
                region = PS[:, r * 1024:(r + 1) * 1024]
                cols = slice(th * HALF, (th + 1) * HALF)
                if on("tensor"):
                    self.wge("wld%d" % (d % 2), base_wld[d % 2] + 16 * (d // 2 + 1))
                    self.wge("bft%d" % (d % 2), base_bft[d % 2] + 16 * (d // 2 + 1))
                    if idx == 0:
                        self.wge("xaev", self.C["xaev"])
                    if idx >= 2:
                        self.wge("evd", base_evd + (idx - 1))
                    for kc in range(NKC):
                        st = (kc == 0)
                        lhsT = WS[:, d % 2, kc, :]
                        rhs = self.arena(self.U1 if th == 0 else self.U2, kc)
                        eng.matmul(region[:, 0:512], lhsT, rhs[:, 0:512], start=st, stop=False)
                        eng.matmul(region[:, 512:1024], lhsT, rhs[:, 512:1024],
                                   start=st, stop=False)
                    for ec in range(2):
                        sp = (ec == 1)
                        lhsT = BFT[:, d % 2, ec, :]
                        xa0 = (self.SCB[:, th * 1024:th * 1024 + 512] if ec == 0
                               else self.MSK[:, th * 2, :])
                        xa1 = (self.SCB[:, th * 1024 + 512:(th + 1) * 1024] if ec == 0
                               else self.MSK[:, th * 2 + 1, :])
                        eng.matmul(region[:, 0:512], lhsT, xa0, start=False, stop=sp)
                        mmi = eng.matmul(region[:, 512:1024], lhsT, xa1, start=False, stop=sp)
                    INC(mmi, "mm2", 1)
                else:
                    INC(None, "mm2", 1)
                xs = XST[:, r * 1024:(r + 1) * 1024]
                if on("sync"):
                    if idx == 0:
                        self.wge("p1bc", self.C["p1bc"])
                        self.wge("p1b", self.C["p1b"])
                    if idx >= 2:
                        self.wge("evd", base_evd + (idx - 1))
                    INC(eng.dma_start(xs, self.xprime[d * 128:(d + 1) * 128, cols]), "xst%d" % r, 16)
                else:
                    INC(None, "xst%d" % r, 16)
                ev = U4[:, r * 1024:(r + 1) * 1024]
                if on("vector"):
                    self.wge("mm2", base_mm2 + (idx + 1))
                    self.wge("xst%d" % r, base_xst[r] + 16 * (idx // 2 + 1))
                    if idx >= 2:
                        self.wge("out%d" % r, base_out[r] + 16 * (idx // 2))
                    INC(eng.scalar_tensor_tensor(ev, region, self.BIAS[:, d:d + 1], xs,
                                                 ALU.add, ALU.add), "evd", 1)
                else:
                    INC(None, "evd", 1)
                if on("sync"):
                    self.wge("evd", base_evd + (idx + 1))
                    INC(eng.dma_start(self.out_fm[d * 128:(d + 1) * 128, cols], ev), "out%d" % r, 16)
                else:
                    INC(None, "out%d" % r, 16)
                idx += 1


def _consts():
    maskc = np.zeros((128, 4, 512), dtype=np.float32)
    for i in range(4):
        k = np.arange(128)[:, None] + i * 128
        q = np.arange(512)[None, :]
        maskc[:, i, :] = np.where(k > q, NEG, 0.0).astype(np.float32)
    emat = np.zeros((16, 256), dtype=np.float32)
    for p in range(128):
        emat[p // 16, p] = 1.0
        emat[8 + p // 16, 128 + p] = 1.0
    identc = np.eye(128, dtype=np.float32)
    return maskc, emat, identc


def kernel(L0_attn_x, L0_attn_Wq, L0_attn_Wk, L0_attn_Wv, L0_attn_Wo,
           L0_attn_cos, L0_attn_sin, L0_ffn_W, L0_ffn_bias, L0_ffn_A,
           L0_ffn_Bf, L0_ffn_coeffs, L0_ffn_scales):
    x = np.asarray(L0_attn_x, dtype=np.float32)
    wq = np.ascontiguousarray(np.asarray(L0_attn_Wq, dtype=np.float32))
    wk = np.ascontiguousarray(np.asarray(L0_attn_Wk, dtype=np.float32))
    wv = np.ascontiguousarray(np.asarray(L0_attn_Wv, dtype=np.float32))
    wo = np.ascontiguousarray(np.asarray(L0_attn_Wo, dtype=np.float32))
    wf = np.ascontiguousarray(np.asarray(L0_ffn_W, dtype=np.float32).T)
    cosT = np.ascontiguousarray(np.asarray(L0_attn_cos, dtype=np.float32)[:S].T)
    sinT = np.ascontiguousarray(np.asarray(L0_attn_sin, dtype=np.float32)[:S].T)
    a_t = np.ascontiguousarray(np.asarray(L0_ffn_A, dtype=np.float32).reshape(256, D).T)
    bf_r = np.ascontiguousarray(np.asarray(L0_ffn_Bf, dtype=np.float32).transpose(0, 2, 1).reshape(256, D))
    bias2d = np.ascontiguousarray(np.asarray(L0_ffn_bias, dtype=np.float32).reshape(32, 128).T)
    coef = np.ascontiguousarray(np.asarray(L0_ffn_coeffs, dtype=np.float32).reshape(16, 1))
    scal = np.ascontiguousarray(np.asarray(L0_ffn_scales, dtype=np.float32).reshape(16, 1))
    maskc, emat, identc = _consts()

    common = dict(wq=wq, wk=wk, wv=wv, wo=wo, wf=wf, cosT=cosT, sinT=sinT,
                  a_t=a_t, bf_r=bf_r, bias2d=bias2d, coef=coef, scal=scal,
                  emat=emat, identc=identc, maskc=maskc)
    in_maps = []
    for c in range(8):
        m = dict(common)
        m["x_fm"] = np.ascontiguousarray(x[c].T)
        in_maps.append(m)

    b = B(upto=6, debug=False)
    res = run_bass_kernel_spmd(b.nc, in_maps, core_ids=list(range(8)))
    out = np.stack([np.asarray(res.results[c]["out_fm"]).T for c in range(8)])
    return np.ascontiguousarray(out)
